# version 04
# baseline (speedup 1.0000x reference)
import contextlib
import numpy as np

import concourse.bass as bass
import concourse.mybir as mybir
from concourse.bass_utils import run_bass_kernel_spmd

F32 = mybir.dt.float32
BF16 = mybir.dt.bfloat16
AF = mybir.ActivationFunctionType
ALU = mybir.AluOpType

D = 2048
DC = 16
TOWN = 2048
NT = 4
S = 4096
NSC = 32
NFC = 64
KVH = 4
EPS_ATTN = 1e-6
EPS_LN = 1e-5
NW = float(8192 * 2048)
CBAR = float(np.float32(0.7978845608))
K1024C = float(np.float32(0.7978845608)) * 1024.0
C24 = float(2.0 ** -24)
C07 = float(np.float32(0.7))
CD7 = float(0.7 - float(np.float32(0.7)))

ENGINES = ("sync", "gpsimd", "vector", "scalar", "tensor")


class Sem:
    __slots__ = ("name", "handle", "producer", "count", "is_dma")

    def __init__(self, name, handle, producer, is_dma):
        self.name, self.handle, self.producer = name, handle, producer
        self.count, self.is_dma = 0, is_dma


class Prog:

    def __init__(self, nc, es):
        self.nc = nc
        self.es = es
        self.ops = {e: [] for e in ENGINES}
        self.watermark = {e: {} for e in ENGINES}

    def sem(self, name, producer, is_dma=False):
        h = self.es.enter_context(self.nc.semaphore(name))
        return Sem(name, h, producer, is_dma)

    def op(self, eng, fn, waits=(), inc=None, inc_amt=1):
        wl = []
        for tok in waits:
            if tok is None:
                continue
            s, v = tok
            if v is None or v <= 0:
                continue
            if s.producer == eng and not s.is_dma:
                continue
            assert v <= s.count, (
                f"deadlock: {eng} waits {s.name}>={v} but only {s.count} emitted")
            if self.watermark[eng].get(s.name, 0) >= v:
                continue
            self.watermark[eng][s.name] = v
            wl.append((s.handle, v))
        tok = None
        if inc is not None:
            assert inc.producer == eng, (eng, inc.name, inc.producer)
            inc.count += inc_amt
            tok = (inc, inc.count)
        self.ops[eng].append((fn, wl, (inc.handle, inc_amt) if inc else None))
        return tok

    def replay(self):
        with self.nc.Block() as block:
            def mk(ename):
                def go(eng):
                    for fn, waits, inc in self.ops[ename]:
                        for h, v in waits:
                            eng.wait_ge(h, v)
                        ins = eng.nop() if fn is None else fn(eng)
                        if inc is not None:
                            ins.then_inc(*inc)
                return go
            block.sync(mk("sync"))
            block.gpsimd(mk("gpsimd"))
            block.vector(mk("vector"))
            block.scalar(mk("scalar"))
            block.tensor(mk("tensor"))


def mx(*toks):
    best = None
    for t in toks:
        if t is None:
            continue
        if best is None or (t[0] is best[0] and t[1] > best[1]):
            best = t
        elif t[0] is not best[0]:
            raise ValueError("mx() needs same-sem tokens")
    return best


_DEBUG = False


def build_nc():
    nc = bass.Bass("TRN2", target_bir_lowering=False, debug=False)
    es = contextlib.ExitStack()
    _dbg_kind = "ExternalOutput" if _DEBUG else "Internal"

    xt = nc.dram_tensor("xt", [D, S], F32, kind="ExternalInput").ap()
    wq = nc.dram_tensor("wq", [16, 128, 2048], F32, kind="ExternalInput").ap()
    wk = nc.dram_tensor("wk", [D, 512], F32, kind="ExternalInput").ap()
    wv = nc.dram_tensor("wv", [D, 512], F32, kind="ExternalInput").ap()
    wo = nc.dram_tensor("wo", [16, 128, 2048], F32, kind="ExternalInput").ap()
    w1 = nc.dram_tensor("w1", [64, 128, 2048], F32, kind="ExternalInput").ap()
    w2 = nc.dram_tensor("w2", [16, 128, 8192], F32, kind="ExternalInput").ap()
    bq = nc.dram_tensor("bq", [128, 16], F32, kind="ExternalInput").ap()
    bk = nc.dram_tensor("bk", [1, 512], F32, kind="ExternalInput").ap()
    bv = nc.dram_tensor("bv", [1, 512], F32, kind="ExternalInput").ap()
    b1 = nc.dram_tensor("b1", [128, 64], F32, kind="ExternalInput").ap()
    b2 = nc.dram_tensor("b2", [128, 16], F32, kind="ExternalInput").ap()
    out = nc.dram_tensor("out", [D, TOWN], F32, kind="ExternalOutput").ap()
    w1q = nc.dram_tensor("w1q", [64, 128, 2048], BF16, kind=_dbg_kind).ap()
    w2q = nc.dram_tensor("w2q", [16, 128, 8192], BF16, kind=_dbg_kind).ap()
    y1s = nc.dram_tensor("y1s", [NT, DC, 128, 512], F32, kind=_dbg_kind).ap()
    if _DEBUG:
        dbg_kv = nc.dram_tensor("dbg_kv", [128, 4 * 129], BF16,
                                kind="ExternalOutput").ap()
        dbg_thr = nc.dram_tensor("dbg_thr", [128, 4], F32,
                                 kind="ExternalOutput").ap()
        dbg_sq = nc.dram_tensor("dbg_sq", [128, 2048], BF16,
                                kind="ExternalOutput").ap()
        dbg_at = nc.dram_tensor("dbg_at", [128, 2048], BF16,
                                kind="ExternalOutput").ap()
        dbg_a1 = nc.dram_tensor("dbg_a1", [128, 1], F32,
                                kind="ExternalOutput").ap()
        dbg_a2 = nc.dram_tensor("dbg_a2", [128, 1], F32,
                                kind="ExternalOutput").ap()
        dbg_tot = nc.dram_tensor("dbg_tot", [1, 4], F32,
                                 kind="ExternalOutput").ap()
        dbg_row4 = nc.dram_tensor("dbg_row4", [1, 4], F32,
                                  kind="ExternalOutput").ap()

    A = nc.alloc_sbuf_tensor("arnA", [128, 32768], BF16)
    B = nc.alloc_sbuf_tensor("arnB", [128, 32768], BF16)
    Bf = B.bitcast(F32)
    C = nc.alloc_sbuf_tensor("arnC", [128, 16384], BF16)
    E = nc.alloc_sbuf_tensor("arnE", [128, 10 * 512], F32)
    Eb = E.bitcast(BF16)
    Fst = nc.alloc_sbuf_tensor("arnF", [128, 4 * 512], F32)
    Gsq = nc.alloc_sbuf_tensor("arnG", [128, 2048], BF16)
    rows = nc.alloc_sbuf_tensor("rows", [1, 6 * 512], F32)
    kvsb = nc.alloc_sbuf_tensor("kvsb", [128, 4 * 129], BF16)
    bq_sb = nc.alloc_sbuf_tensor("bq_sb", [128, 16], F32)
    b1_sb = nc.alloc_sbuf_tensor("b1_sb", [128, 64], F32)
    b2_sb = nc.alloc_sbuf_tensor("b2_sb", [128, 16], F32)
    thr = nc.alloc_sbuf_tensor("thr", [128, 4], F32)
    acc1 = nc.alloc_sbuf_tensor("acc1", [128, 1], F32)
    acc2 = nc.alloc_sbuf_tensor("acc2", [128, 1], F32)
    acc_t = nc.alloc_sbuf_tensor("acc_t", [128, 2], F32)
    ones_bf = nc.alloc_sbuf_tensor("ones_bf", [128, 1], BF16)
    ones_m1 = nc.alloc_sbuf_tensor("ones_m1", [128, 1], F32)
    ones_k1 = nc.alloc_sbuf_tensor("ones_k1", [1, 128], F32)
    c_nhalf = nc.alloc_sbuf_tensor("c_nhalf", [1, 1], F32)
    c_none = nc.alloc_sbuf_tensor("c_none", [1, 1], F32)
    c_eps_at = nc.alloc_sbuf_tensor("c_eps_at", [1, 1], F32)
    c_eps_ln = nc.alloc_sbuf_tensor("c_eps_ln", [1, 1], F32)
    c_half = nc.alloc_sbuf_tensor("c_half", [128, 1], F32)
    tot_sb = nc.alloc_sbuf_tensor("tot_sb", [1, 4], F32)
    PB = [nc.alloc_psum_tensor(f"PB{i}", [128, 512], F32) for i in range(8)]

    Aap, Bbf, Cap, Gap, Bfap = A.ap(), B.ap(), C.ap(), Gsq.ap(), Bf.ap()

    def Esl(i, n=1):
        return E.ap()[:, i * 512:(i + n) * 512]

    def Ebsl(i):
        return Eb.ap()[:, i * 1024:(i + 1) * 1024]

    def row(i):
        return rows.ap()[0:1, i * 512:(i + 1) * 512]

    p = Prog(nc, es)
    NLANE = 6
    sS = [p.sem(f"sS{i}", "sync", is_dma=True) for i in range(NLANE)]
    sG = [p.sem(f"sG{i}", "gpsimd", is_dma=True) for i in range(NLANE)]
    sT = p.sem("sT", "tensor")
    sV = p.sem("sV", "vector")
    sC = p.sem("sC", "scalar")
    _qstate = {"sync": {"rr": 0, "lanes": sS, "last": [None] * NLANE},
               "gpsimd": {"rr": 0, "lanes": sG, "last": [None] * NLANE}}

    def dma(qeng, _unused, dst, src, waits=()):
        q = _qstate[qeng]
        i = q["rr"]
        q["rr"] = (i + 1) % NLANE
        wl = list(waits) + [q["last"][i]]
        tok = p.op(qeng, lambda e, a=dst, b=src: e.dma_start(out=a, in_=b),
                   waits=wl, inc=q["lanes"][i], inc_amt=16)
        q["last"][i] = tok
        return tok

    st_ = {}
    for nm, t_, val in (("ones_bf", ones_bf, 1.0), ("ones_m1", ones_m1, 1.0),
                        ("ones_k1", ones_k1, 1.0), ("c_nhalf", c_nhalf, -0.5),
                        ("c_none", c_none, -1.0), ("c_eps_at", c_eps_at, EPS_ATTN),
                        ("c_eps_ln", c_eps_ln, EPS_LN), ("c_half", c_half, 0.5),
                        ("acc1", acc1, 0.0), ("acc2", acc2, 0.0)):
        st_[nm] = p.op("vector", lambda e, a=t_.ap(), v=val: e.memset(a, v), inc=sV)
    t_bq = dma("sync", sS, bq_sb.ap(), bq)
    t_b1 = dma("sync", sS, b1_sb.ap(), b1)
    t_b2 = dma("sync", sS, b2_sb.ap(), b2)
    t_bkr = dma("sync", sS, row(4), bk)
    t_bvr = dma("sync", sS, row(5), bv)
    t_bkmm = p.op("tensor", lambda e: e.matmul(PB[6].ap(), ones_k1.ap(), row(4),
                                               start=True, stop=True),
                  waits=[t_bkr, st_["ones_k1"]], inc=sT)
    t_bvmm = p.op("tensor", lambda e: e.matmul(PB[7].ap(), ones_k1.ap(), row(5),
                                               start=True, stop=True),
                  waits=[t_bvr], inc=sT)
    bkbc, bvbc = Esl(3), Esl(4)
    t_bkbc = p.op("vector", lambda e: e.tensor_copy(bkbc, PB[6].ap()),
                  waits=[t_bkmm], inc=sV)
    t_bvbc = p.op("vector", lambda e: e.tensor_copy(bvbc, PB[7].ap()),
                  waits=[t_bvmm], inc=sV)

    t_xown, t_xpart = [], []
    for dc in range(DC):
        t_xown.append(dma("gpsimd", sG, Aap[:, dc * 2048:(dc + 1) * 2048],
                          xt[dc * 128:(dc + 1) * 128, 0:TOWN]))
    wkT = Cap[:, 0:8192].rearrange("p (dc e) -> p dc e", e=512)
    wvT = Cap[:, 8192:16384].rearrange("p (dc e) -> p dc e", e=512)
    t_wk = dma("gpsimd", sG, wkT, wk.rearrange("(dc p) e -> p dc e", p=128))
    t_wv = dma("gpsimd", sG, wvT, wv.rearrange("(dc p) e -> p dc e", p=128))
    for dc in range(DC):
        t_xpart.append(dma("gpsimd", sG, Bbf[:, dc * 2048:(dc + 1) * 2048],
                           xt[dc * 128:(dc + 1) * 128, TOWN:S]))

    scale_jobs = [("w1", w1[fc, :, h2 * 1024:(h2 + 1) * 1024])
                  for fc in range(64) for h2 in range(2)]
    scale_jobs += [("w2", w2[dc, :, e8 * 1024:(e8 + 1) * 1024])
                   for dc in range(16) for e8 in range(8)]
    sst = {"i": 0, "abs": [None, None], "acc": {"w1": st_["acc1"], "w2": st_["acc2"]},
           "done": False}

    def emit_scale(njobs):
        for _ in range(njobs):
            i = sst["i"]
            if i >= len(scale_jobs):
                sst["done"] = True
                return
            sst["i"] += 1
            which, src = scale_jobs[i]
            sl = i % 2
            stage = Esl(5 + 2 * sl, 2)
            acol = acc_t.ap()[:, sl:sl + 1]
            tok_d = dma("sync", sS, stage, src, waits=[sst["abs"][sl]])
            tok_a = p.op("scalar",
                         lambda e, s2=stage, d2=Ebsl(9), a2=acol:
                         e.activation(d2, s2, AF.Abs, accum_out=a2),
                         waits=[tok_d, sst["acc"]["w1"], sst["acc"]["w2"]],
                         inc=sC)
            sst["abs"][sl] = tok_a
            accd = acc1.ap() if which == "w1" else acc2.ap()
            sst["acc"][which] = p.op(
                "vector", lambda e, a2=accd, c2=acol:
                e.scalar_tensor_tensor(a2, c2, K1024C, a2, ALU.subtract, ALU.add),
                waits=[tok_a], inc=sV)
        if sst["i"] >= len(scale_jobs):
            sst["done"] = True

    kfree = [None, None]
    vfree = [None, None]
    skvfree = [None, None]
    kv_last = None
    for sc in range(NSC):
        par = sc % 2
        own = sc < 16
        base = (sc if own else sc - 16) * 128
        src = Aap if own else Bbf
        xw = (t_xown if own else t_xpart)
        xtiles = [src[:, dc * 2048 + base: dc * 2048 + base + 128]
                  for dc in range(DC)]
        tk = tv = None
        for dc in range(DC):
            tk = p.op("tensor",
                      lambda e, o=PB[par].ap(), l=xtiles[dc], r=wkT[:, dc, :],
                      s=(dc == 0), z=(dc == DC - 1):
                      e.matmul(o, l, r, start=s, stop=z),
                      waits=[xw[dc], t_wk, kfree[par]],
                      inc=sT if dc == DC - 1 else None)
        for dc in range(DC):
            tv = p.op("tensor",
                      lambda e, o=PB[2 + par].ap(), l=xtiles[dc], r=wvT[:, dc, :],
                      s=(dc == 0), z=(dc == DC - 1):
                      e.matmul(o, l, r, start=s, stop=z),
                      waits=[t_wv, vfree[par]],
                      inc=sT if dc == DC - 1 else None)
        sk_t = Gap[:, par * 512:(par + 1) * 512]
        sv_t = Gap[:, 1024 + par * 512: 1024 + (par + 1) * 512]
        t1 = p.op("vector", lambda e, o=Esl(0), a=PB[par].ap(), b=bkbc:
                  e.tensor_add(o, a, b), waits=[tk, t_bkbc], inc=sV)
        t2 = p.op("vector", lambda e, o=Esl(1), a=Esl(0):
                  e.tensor_scalar_min(o, a, 0.0), inc=sV)
        t3 = p.op("vector", lambda e, o=Esl(2), a=Esl(0):
                  e.tensor_scalar_max(o, a, 0.0), inc=sV)
        kfree[par] = t3
        t4 = p.op("scalar", lambda e, o=Esl(1), a=Esl(1):
                  e.activation(o, a, AF.Exp), waits=[t2], inc=sC)
        t5 = p.op("vector", lambda e, o=sk_t, a=Esl(1), b=Esl(2):
                  e.tensor_add(o, a, b), waits=[t4, skvfree[par]], inc=sV)
        t6 = p.op("vector", lambda e, o=sv_t, a=PB[2 + par].ap(), b=bvbc:
                  e.scalar_tensor_tensor(o, a, 1.0, b, ALU.mult, ALU.add),
                  waits=[tv, t_bvbc], inc=sV)
        vfree[par] = t6
        tkv = None
        for h in range(KVH):
            p.op("tensor",
                 lambda e, o=PB[4 + h].ap()[:, 0:128],
                 l=sk_t[:, h * 128:(h + 1) * 128],
                 r=sv_t[:, h * 128:(h + 1) * 128],
                 s=(sc == 0), z=(sc == NSC - 1):
                 e.matmul(o, l, r, start=s, stop=z),
                 waits=[t5, t6])
            tkv = p.op("tensor",
                       lambda e, o=PB[4 + h].ap()[:, 128:129],
                       l=sk_t[:, h * 128:(h + 1) * 128], r=ones_bf.ap(),
                       z=(sc == NSC - 1):
                       e.matmul(o, l, r, start=False, stop=z),
                       waits=[st_["ones_bf"]],
                       inc=sT if h == KVH - 1 else None)
        skvfree[par] = tkv
        kv_last = tkv
        emit_scale(2)

    t_kv_ev = None
    for h in range(KVH):
        t_kv_ev = p.op("vector",
                       lambda e, o=kvsb.ap()[:, h * 129:(h + 1) * 129],
                       a=PB[4 + h].ap()[:, 0:129]: e.tensor_copy(o, a),
                       waits=[kv_last], inc=sV)
    if _DEBUG:
        dma("gpsimd", sG, dbg_kv, kvsb.ap(), waits=[t_kv_ev])

    SLN, SQN, S2N, SQ2N = (Fst.ap()[:, i * 512:(i + 1) * 512] for i in range(4))
    def wq_buf(i):
        return Cap[:, (i % 2) * 2048:((i % 2) + 1) * 2048]

    def wo_buf(i):
        return Cap[:, 4096 + (i % 2) * 2048: 4096 + ((i % 2) + 1) * 2048]

    wq_tok = [None] * 16
    for i in range(2):
        wq_tok[i] = dma("gpsimd", sG, wq_buf(i), wq[i], waits=[kv_last])

    q_free = [None, None]
    qkv_free = nrm_free = rbc_free = sq_free = None
    attnT_tok = [None] * 16
    uro = [None, None]
    last_q_mm = [None] * 16
    dbg_gate = None
    for ic in range(16):
        sq_toks = []
        for t in range(NT):
            par = (ic * NT + t) % 2
            tq = None
            for dc in range(DC):
                tq = p.op("tensor",
                          lambda e, o=PB[par].ap(),
                          l=wq_buf(ic)[:, dc * 128:(dc + 1) * 128],
                          r=Aap[:, dc * 2048 + t * 512: dc * 2048 + (t + 1) * 512],
                          s=(dc == 0), z=(dc == DC - 1):
                          e.matmul(o, l, r, start=s, stop=z),
                          waits=[wq_tok[ic], q_free[par],
                                 sq_free if t == 0 else None,
                                 dbg_gate if t == 0 else None],
                          inc=sT if dc == DC - 1 else None)
            bqc = bq_sb.ap()[:, ic:ic + 1]
            sq_t = Gap[:, t * 512:(t + 1) * 512]
            m1_ = p.op("vector", lambda e, o=Esl(1), a=PB[par].ap(), b=bqc:
                       e.tensor_scalar(o, a, b, 0.0, ALU.add, ALU.min),
                       waits=[tq, t_bq], inc=sV)
            m2_ = p.op("vector", lambda e, o=Esl(2), a=PB[par].ap(), b=bqc:
                       e.tensor_scalar(o, a, b, 0.0, ALU.add, ALU.max),
                       inc=sV)
            q_free[par] = m2_
            m3_ = p.op("scalar", lambda e, o=Esl(1), a=Esl(1):
                       e.activation(o, a, AF.Exp), waits=[m1_], inc=sC)
            m4_ = p.op("vector", lambda e, o=sq_t, a=Esl(1), b=Esl(2):
                       e.tensor_add(o, a, b), waits=[m3_], inc=sV)
            sq_toks.append(m4_)
        last_q_mm[ic] = tq
        if _DEBUG and ic == 0:
            dbg_gate = dma("gpsimd", sG, dbg_sq, Gap, waits=sq_toks)
        if ic + 2 < 16:
            wq_tok[ic + 2] = dma("gpsimd", sG, wq_buf(ic + 2), wq[ic + 2],
                                 waits=[tq])
        kvh = ic // 4
        kv_l = kvsb.ap()[:, kvh * 129: kvh * 129 + 128]
        ks_l = kvsb.ap()[:, kvh * 129 + 128: kvh * 129 + 129]
        for t in range(NT):
            sq_t = Gap[:, t * 512:(t + 1) * 512]
            tqkv = p.op("tensor", lambda e, o=PB[2].ap(), l=kv_l, r=sq_t:
                        e.matmul(o, l, r, start=True, stop=True),
                        waits=[sq_toks[t], t_kv_ev, qkv_free], inc=sT)
            tnrm = p.op("tensor", lambda e, o=PB[3].ap()[0:1, :], l=ks_l, r=sq_t:
                        e.matmul(o, l, r, start=True, stop=True),
                        waits=[nrm_free], inc=sT)
            j = (ic * NT + t) % 2
            u_r, r_r = row(0 + j), row(2 + j)
            tu = p.op("scalar", lambda e, o=u_r, a=PB[3].ap()[0:1, :],
                      b=c_eps_at.ap():
                      e.activation(o, a, AF.Ln, bias=b),
                      waits=[tnrm, uro[j], st_["c_eps_at"]], inc=sC)
            nrm_free = tu
            tr = p.op("scalar", lambda e, o=r_r, a=u_r, s2=c_none.ap():
                      e.activation(o, a, AF.Exp, scale=s2),
                      waits=[st_["c_none"]], inc=sC)
            tbc = p.op("tensor", lambda e, o=PB[4].ap(), l=ones_k1.ap(), r=r_r:
                       e.matmul(o, l, r, start=True, stop=True),
                       waits=[tr, rbc_free], inc=sT)
            uro[j] = tbc
            tcp = p.op("vector", lambda e, o=Esl(0), a=PB[4].ap():
                       e.tensor_copy(o, a), waits=[tbc], inc=sV)
            rbc_free = tcp
            tdv = p.op("vector",
                       lambda e,
                       o=Bbf[:, ic * 2048 + t * 512: ic * 2048 + (t + 1) * 512],
                       a=PB[2].ap(), b=Esl(0):
                       e.tensor_mul(o, a, b), waits=[tqkv], inc=sV)
            qkv_free = tdv
        attnT_tok[ic] = tdv
        sq_free = tdv
        if _DEBUG and ic == 0:
            sst["dbg_at"] = dma("gpsimd", sG, dbg_at, Bbf[:, 0:2048],
                                waits=[tdv])
        emit_scale(8)

    wo_tok = [None] * 16
    for i in range(2):
        wo_tok[i] = dma("gpsimd", sG, wo_buf(i), wo[i], waits=[kv_last])
    wo_free = [None, None]
    xres_tok = [None, None]
    xres_free = [None, None]
    y_free = [None, None]
    y1_wr = [[None] * DC for _ in range(NT)]
    wo_mm_done = None
    thr_done = None
    for dc in range(DC):
        for t in range(NT):
            par = (dc * NT + t) % 2
            two = None
            for icc in range(16):
                two = p.op("tensor",
                           lambda e, o=PB[5 + par].ap(),
                           l=wo_buf(dc)[:, icc * 128:(icc + 1) * 128],
                           r=Bbf[:, icc * 2048 + t * 512: icc * 2048 + (t + 1) * 512],
                           s=(icc == 0), z=(icc == 15):
                           e.matmul(o, l, r, start=s, stop=z),
                           waits=[wo_tok[dc], attnT_tok[icc], wo_free[par]],
                           inc=sT if icc == 15 else None)
            wo_mm_done = two
            xs = (dc * NT + t) % 2
            xres = Esl(3 + xs)
            xres_tok[xs] = dma("gpsimd", sG, xres,
                               xt[dc * 128:(dc + 1) * 128, t * 512:(t + 1) * 512],
                               waits=[xres_free[xs],
                                      t_bkbc if (dc == 0 and t < 2) else None,
                                      t_bvbc if (dc == 0 and t < 2) else None])
            ys = (dc * NT + t) % 2
            ybuf = Esl(0 + ys)
            ty = p.op("vector", lambda e, o=ybuf, a=PB[5 + par].ap(), b=xres:
                      e.tensor_add(o, a, b),
                      waits=[two, xres_tok[xs], y_free[ys]], inc=sV)
            xres_free[xs] = ty
            wo_free[par] = ty
            y1_wr[t][dc] = dma("gpsimd", sG, y1s[t, dc], ybuf, waits=[ty])
            y_free[ys] = y1_wr[t][dc]
        wo_mm_last_for_buf = wo_mm_done
        if dc + 2 < 16:
            wo_tok[dc + 2] = dma("gpsimd", sG, wo_buf(dc + 2), wo[dc + 2],
                                 waits=[wo_mm_last_for_buf])
        emit_scale(4)
        if sst["done"] and thr_done is None:
            slk0 = p.op("vector", lambda e, o=SLN: e.tensor_copy(o, o),
                        waits=[sst["acc"]["w1"], sst["acc"]["w2"]], inc=sV)
            tw1 = p.op("tensor", lambda e, o=PB[7].ap()[0:1, 0:1],
                       l=ones_m1.ap(), r=acc1.ap():
                       e.matmul(o, l, r, start=True, stop=True),
                       waits=[slk0, t_kv_ev, st_["ones_m1"]], inc=sT)
            tw2 = p.op("tensor", lambda e, o=PB[7].ap()[0:1, 4:5],
                       l=ones_m1.ap(), r=acc2.ap():
                       e.matmul(o, l, r, start=True, stop=True), inc=sT)
            p.op("vector", lambda e, o=row(5)[0:1, 0:1], a=PB[7].ap()[0:1, 0:1]:
                 e.tensor_scalar(o, a, C24, CBAR, ALU.mult, ALU.add),
                 waits=[tw1, tw2], inc=sV)
            p.op("vector", lambda e, o=row(5)[0:1, 1:2], a=PB[7].ap()[0:1, 4:5]:
                 e.tensor_scalar(o, a, C24, CBAR, ALU.mult, ALU.add), inc=sV)
            p.op("vector", lambda e, o=SLN: e.tensor_copy(o, o), inc=sV)
            p.op("vector", lambda e, o=row(4)[0:1, 1:2], a=row(5)[0:1, 0:1]:
                 e.tensor_scalar_mul(o, a, C07), inc=sV)
            p.op("vector", lambda e, o=row(4)[0:1, 3:4], a=row(5)[0:1, 1:2]:
                 e.tensor_scalar_mul(o, a, C07), inc=sV)
            p.op("vector", lambda e, o=row(5)[0:1, 2:3], a=row(5)[0:1, 0:1]:
                 e.tensor_scalar_mul(o, a, CD7), inc=sV)
            p.op("vector", lambda e, o=row(5)[0:1, 3:4], a=row(5)[0:1, 1:2]:
                 e.tensor_scalar_mul(o, a, CD7), inc=sV)
            p.op("vector", lambda e, o=SLN: e.tensor_copy(o, o), inc=sV)
            p.op("vector", lambda e, o=row(4)[0:1, 1:2], a=row(4)[0:1, 1:2],
                 b=row(5)[0:1, 2:3]: e.tensor_add(o, a, b), inc=sV)
            p.op("vector", lambda e, o=row(4)[0:1, 3:4], a=row(4)[0:1, 3:4],
                 b=row(5)[0:1, 3:4]: e.tensor_add(o, a, b), inc=sV)
            p.op("vector", lambda e, o=SLN: e.tensor_copy(o, o), inc=sV)
            p.op("vector", lambda e, o=row(4)[0:1, 0:1], a=row(4)[0:1, 1:2]:
                 e.tensor_scalar_mul(o, a, -1.0), inc=sV)
            p.op("vector", lambda e, o=row(4)[0:1, 2:3], a=row(4)[0:1, 3:4]:
                 e.tensor_scalar_mul(o, a, -1.0), inc=sV)
            p.op("vector", lambda e, o=SLN: e.tensor_copy(o, o), inc=sV)
            slk1 = p.op("vector", lambda e, o=SLN: e.tensor_copy(o, o), inc=sV)
            tbc2 = p.op("tensor", lambda e, o=PB[7].ap()[:, 8:12],
                        l=ones_k1.ap(), r=row(4)[0:1, 0:4]:
                        e.matmul(o, l, r, start=True, stop=True),
                        waits=[slk1, st_["ones_k1"]], inc=sT)
            thr_done = p.op("vector", lambda e, o=thr.ap(), a=PB[7].ap()[:, 8:12]:
                            e.tensor_copy(o, a), waits=[tbc2], inc=sV)
    assert thr_done is not None
    if _DEBUG:
        dma("gpsimd", sG, dbg_thr, thr.ap(), waits=[thr_done])
        tac = p.op("vector", lambda e, o=tot_sb.ap()[0:1, 0:4],
                   a=tot_sb.ap()[0:1, 0:4]: e.tensor_copy(o, a),
                   waits=[thr_done], inc=sV)
        dma("gpsimd", sG, dbg_a1, acc1.ap(), waits=[thr_done])
        dma("gpsimd", sG, dbg_a2, acc2.ap(), waits=[thr_done])
        dma("gpsimd", sG, dbg_tot, tot_sb.ap(), waits=[thr_done])
        dma("gpsimd", sG, dbg_row4, row(4)[0:1, 0:4], waits=[thr_done])


    h_free = [None, None]
    out_free = [None, None]
    bc_free = None
    st_free = None
    w1q_tok = [None, None]
    w1q_free = [None, None]
    w1q_gwr = [None, None]
    w2q_tok = [None, None, None]
    w2q_free = [None, None, None]
    w2q_gwr = [None, None, None]
    qs_free = [mx(sst["abs"][0], sst["abs"][1]), mx(sst["abs"][0], sst["abs"][1])]
    sg_tok = mx(sst["abs"][0], sst["abs"][1])
    x1_ready = [None] * DC
    x1bf_ready = [None] * DC
    x1_free = [None] * DC
    hbf_tok = [None] * NFC
    w1q_wr = [None] * NFC
    w2q_wrp = [[None] * 8 for _ in range(16)]
    out_last = None

    def w1q_buf(i):
        return Cap[:, (i % 2) * 2048:((i % 2) + 1) * 2048]

    def w2q_slot(i):
        return Cap[:, 4096 + (i % 3) * 4096: 4096 + ((i % 3) + 1) * 4096]

    for t in range(NT):
        x1_all = Bfap[:, 0:8192].rearrange("p (dc c) -> p dc c", c=512)
        wl = [y1_wr[t][dc] for dc in range(DC)] + list(x1_free)
        if t == 0:
            wl.append(wo_mm_done)
            if _DEBUG:
                wl.append(sst.get("dbg_at"))
        t_yld = dma("sync", sS, x1_all, y1s[t].rearrange("dc p c -> p dc c"),
                    waits=wl)
        tz1 = p.op("vector", lambda e, o=SLN: e.memset(o, 0.0),
                   waits=[st_free], inc=sV)
        tz2 = p.op("vector", lambda e, o=SQN: e.memset(o, 0.0), inc=sV)
        tacc = None
        tsq_prev = None
        for dc in range(DC):
            x1d = Bfap[:, dc * 512:(dc + 1) * 512]
            p.op("vector", lambda e, o=SLN, a=x1d:
                 e.tensor_add(o, o, a), waits=[t_yld, tz1], inc=sV)
            t2 = p.op("scalar", lambda e, o=Esl(1), a=x1d:
                      e.activation(o, a, AF.Square),
                      waits=[t_yld, tacc], inc=sC)
            tsq_prev = t2
            tacc = p.op("vector", lambda e, o=SQN, a=Esl(1):
                        e.tensor_add(o, o, a), waits=[t2, tz2], inc=sV)
        trs = p.op("tensor", lambda e, o=PB[5].ap()[0:1, :], l=ones_m1.ap(),
                   r=SLN: e.matmul(o, l, r, start=True, stop=True),
                   waits=[tacc, st_free, st_["ones_m1"]], inc=sT)
        tm = p.op("vector", lambda e, o=row(0), a=PB[5].ap()[0:1, :]:
                  e.tensor_scalar_mul(o, a, 1.0 / 2048.0), waits=[trs], inc=sV)
        trq = p.op("tensor", lambda e, o=PB[5].ap()[0:1, :], l=ones_m1.ap(),
                   r=SQN: e.matmul(o, l, r, start=True, stop=True),
                   waits=[tm], inc=sT)
        te2 = p.op("vector", lambda e, o=row(1), a=PB[5].ap()[0:1, :]:
                   e.tensor_scalar_mul(o, a, 1.0 / 2048.0), waits=[trq], inc=sV)
        st_free = te2
        tm2 = p.op("scalar", lambda e, o=row(2), a=row(0):
                   e.activation(o, a, AF.Square), waits=[tm], inc=sC)
        tvv = p.op("vector", lambda e, o=row(1), a=row(1), b=row(2):
                   e.tensor_sub(o, a, b), waits=[tm2], inc=sV)
        tl = p.op("scalar", lambda e, o=row(2), a=row(1), b=c_eps_ln.ap():
                  e.activation(o, a, AF.Ln, bias=b),
                  waits=[tvv, st_["c_eps_ln"]], inc=sC)
        tA = p.op("scalar", lambda e, o=row(3), a=row(2), s2=c_nhalf.ap():
                  e.activation(o, a, AF.Exp, scale=s2),
                  waits=[st_["c_nhalf"]], inc=sC)
        tB1 = p.op("vector", lambda e, o=row(2), a=row(0), b=row(3):
                   e.tensor_mul(o, a, b), waits=[tA, tl], inc=sV)
        tB2 = p.op("vector", lambda e, o=row(2), a=row(2):
                   e.tensor_scalar_mul(o, a, -1.0), inc=sV)
        tslk = p.op("vector", lambda e, o=SLN: e.tensor_copy(o, o), inc=sV)
        tba = p.op("tensor", lambda e, o=PB[4].ap(), l=ones_k1.ap(), r=row(3):
                   e.matmul(o, l, r, start=True, stop=True),
                   waits=[tA, bc_free], inc=sT)
        Abc, Bbc = Esl(2), Esl(9)
        tca = p.op("vector", lambda e, o=Abc, a=PB[4].ap(): e.tensor_copy(o, a),
                   waits=[tba], inc=sV)
        tbb = p.op("tensor", lambda e, o=PB[4].ap(), l=ones_k1.ap(), r=row(2):
                   e.matmul(o, l, r, start=True, stop=True),
                   waits=[tslk, tca], inc=sT)
        tcb = p.op("vector", lambda e, o=Bbc, a=PB[4].ap(): e.tensor_copy(o, a),
                   waits=[tbb, sst["abs"][0] if t == 0 else None,
                          sst["abs"][1] if t == 0 else None], inc=sV)
        bc_free = tcb
        for dc in range(DC):
            x1d = Bfap[:, dc * 512:(dc + 1) * 512]
            x1b = Bbf[:, 16384 + dc * 512: 16384 + (dc + 1) * 512]
            ta1 = p.op("vector", lambda e, o=Esl(0), a=x1d, b=Abc:
                       e.tensor_mul(o, a, b), waits=[tca, tsq_prev], inc=sV)
            ta2 = p.op("vector", lambda e, o=x1d, a=Esl(0), b=Bbc:
                       e.tensor_add(o, a, b), waits=[tcb], inc=sV)
            x1_ready[dc] = ta2
            x1bf_ready[dc] = p.op("vector", lambda e, o=x1b, a=x1d:
                                  e.tensor_copy(o, a), inc=sV)
        for fc in range(NFC):
            buf = w1q_buf(fc)
            par = fc % 2
            if t == 0:
                ptoks = []
                for h2 in range(2):
                    sl = h2
                    stage = Esl(3 + 2 * sl, 2)
                    tqd = dma("sync", sS, stage,
                              w1[fc, :, h2 * 1024:(h2 + 1) * 1024],
                              waits=[qs_free[sl],
                                     t_yld if fc == 0 else None])
                    ts1 = p.op("scalar", lambda e, o=Ebsl(7), a=stage,
                               b=thr.ap()[:, 0:1]:
                               e.activation(o, a, AF.Sign, bias=b),
                               waits=[tqd, thr_done, sg_tok], inc=sC)
                    ts2 = p.op("scalar", lambda e, o=Ebsl(8), a=stage,
                               b=thr.ap()[:, 1:2]:
                               e.activation(o, a, AF.Sign, bias=b), inc=sC)
                    qs_free[sl] = ts2
                    tq0 = p.op("vector", lambda e, o=Ebsl(7), a=Ebsl(7),
                               b=Ebsl(8): e.tensor_add(o, a, b),
                               waits=[ts2], inc=sV)
                    tq = p.op("scalar",
                              lambda e, o=buf[:, h2 * 1024:(h2 + 1) * 1024],
                              a=Ebsl(7): e.activation(o, a, AF.Sign),
                              waits=[tq0, w1q_free[par], w1q_gwr[par]],
                              inc=sC)
                    sg_tok = tq
                    ptoks.append(tq)
                w1q_tok[par] = ptoks[-1]
                w1q_wr[fc] = dma("gpsimd", sG, w1q[fc], buf, waits=ptoks)
                w1q_gwr[par] = w1q_wr[fc]
            else:
                w1q_tok[par] = dma("sync", sS, buf, w1q[fc],
                                   waits=[w1q_free[par], w1q_gwr[par],
                                          w1q_wr[fc]])
                w1q_gwr[par] = None
            th = None
            for dc in range(DC):
                th = p.op("tensor",
                          lambda e, o=PB[par].ap(),
                          l=buf[:, dc * 128:(dc + 1) * 128],
                          r=Bbf[:, 16384 + dc * 512: 16384 + (dc + 1) * 512],
                          s=(dc == 0), z=(dc == DC - 1):
                          e.matmul(o, l, r, start=s, stop=z),
                          waits=[w1q_tok[par], x1bf_ready[dc], h_free[par]],
                          inc=sT if dc == DC - 1 else None)
            w1q_free[par] = th
            hb = Aap[:, fc * 512:(fc + 1) * 512]
            hbf_tok[fc] = p.op("scalar",
                               lambda e, o=hb, a=PB[par].ap(),
                               b=b1_sb.ap()[:, fc:fc + 1]:
                               e.activation(o, a, AF.Gelu, bias=b),
                               waits=[th, t_b1], inc=sC)
            h_free[par] = hbf_tok[fc]
        for dc in range(DC):
            halves = []
            for hh in range(2):
                slot = (dc * 2 + hh) % 3
                sap = w2q_slot(dc * 2 + hh)
                if t == 0:
                    ptoks = []
                    for pc in range(4):
                        col0 = hh * 4096 + pc * 1024
                        sl = pc % 2
                        stage = Esl(3 + 2 * sl, 2)
                        tqd = dma("sync", sS, stage, w2[dc, :, col0:col0 + 1024],
                                  waits=[qs_free[sl]])
                        ts1 = p.op("scalar", lambda e, o=Ebsl(7), a=stage,
                                   b=thr.ap()[:, 2:3]:
                                   e.activation(o, a, AF.Sign, bias=b),
                                   waits=[tqd, sg_tok], inc=sC)
                        ts2 = p.op("scalar", lambda e, o=Ebsl(8), a=stage,
                                   b=thr.ap()[:, 3:4]:
                                   e.activation(o, a, AF.Sign, bias=b), inc=sC)
                        qs_free[sl] = ts2
                        tq0 = p.op("vector", lambda e, o=Ebsl(7), a=Ebsl(7),
                                   b=Ebsl(8): e.tensor_add(o, a, b),
                                   waits=[ts2], inc=sV)
                        tq = p.op("scalar",
                                  lambda e, o=sap[:, pc * 1024:(pc + 1) * 1024],
                                  a=Ebsl(7): e.activation(o, a, AF.Sign),
                                  waits=[tq0, w2q_free[slot], w2q_gwr[slot]],
                                  inc=sC)
                        sg_tok = tq
                        ptoks.append(tq)
                        w2q_wrp[dc][hh * 4 + pc] = dma(
                            "gpsimd", sG, w2q[dc, :, col0:col0 + 1024],
                            sap[:, pc * 1024:(pc + 1) * 1024], waits=[tq])
                    w2q_tok[slot] = ptoks[-1]
                    w2q_gwr[slot] = w2q_wrp[dc][hh * 4 + 3]
                else:
                    w2q_tok[slot] = dma(
                        "sync", sS, sap, w2q[dc, :, hh * 4096:(hh + 1) * 4096],
                        waits=[w2q_free[slot], w2q_gwr[slot],
                               w2q_wrp[dc][hh * 4 + 3]])
                    w2q_gwr[slot] = None
                halves.append((slot, sap, w2q_tok[slot]))
            par = dc % 2
            tmm = None
            for fc in range(NFC):
                slot, sap, wtok = halves[fc // 32]
                tmm = p.op("tensor",
                           lambda e, o=PB[2 + par].ap(),
                           l=sap[:, (fc % 32) * 128:((fc % 32) + 1) * 128],
                           r=Aap[:, fc * 512:(fc + 1) * 512],
                           s=(fc == 0), z=(fc == NFC - 1):
                           e.matmul(o, l, r, start=s, stop=z),
                           waits=[wtok, hbf_tok[fc], out_free[par]],
                           inc=sT if fc % 32 == 31 else None)
                if fc == 31:
                    w2q_free[halves[0][0]] = tmm
            w2q_free[halves[1][0]] = tmm
            x1d = Bfap[:, dc * 512:(dc + 1) * 512]
            tv1 = p.op("vector", lambda e, o=Esl(0), a=PB[2 + par].ap(),
                       b=b2_sb.ap()[:, dc:dc + 1]:
                       e.tensor_scalar(o, a, b, None, ALU.add),
                       waits=[tmm, t_b2], inc=sV)
            out_free[par] = tv1
            tv2 = p.op("vector", lambda e, o=x1d, a=Esl(0), b=x1d:
                       e.tensor_add(o, a, b), waits=[x1_ready[dc]], inc=sV)
            if dc == 0:
                p.op("vector", lambda e, o=S2N: e.memset(o, 0.0), inc=sV)
                p.op("vector", lambda e, o=SQ2N: e.memset(o, 0.0), inc=sV)
            p.op("vector", lambda e, o=S2N, a=x1d:
                 e.tensor_add(o, o, a), waits=[tv2], inc=sV)
            t2 = p.op("scalar", lambda e, o=Esl(1), a=x1d:
                      e.activation(o, a, AF.Square),
                      waits=[tv2, tacc], inc=sC)
            tacc = p.op("vector", lambda e, o=SQ2N, a=Esl(1):
                        e.tensor_add(o, o, a), waits=[t2], inc=sV)
            x1_ready[dc] = (tv2, tacc)
        trs = p.op("tensor", lambda e, o=PB[5].ap()[0:1, :], l=ones_m1.ap(),
                   r=S2N: e.matmul(o, l, r, start=True, stop=True),
                   waits=[tacc, st_free], inc=sT)
        tm = p.op("vector", lambda e, o=row(0), a=PB[5].ap()[0:1, :]:
                  e.tensor_scalar_mul(o, a, 1.0 / 2048.0), waits=[trs], inc=sV)
        trq = p.op("tensor", lambda e, o=PB[5].ap()[0:1, :], l=ones_m1.ap(),
                   r=SQ2N: e.matmul(o, l, r, start=True, stop=True),
                   waits=[tm], inc=sT)
        te2 = p.op("vector", lambda e, o=row(1), a=PB[5].ap()[0:1, :]:
                   e.tensor_scalar_mul(o, a, 1.0 / 2048.0), waits=[trq], inc=sV)
        st_free = te2
        tm2 = p.op("scalar", lambda e, o=row(2), a=row(0):
                   e.activation(o, a, AF.Square), waits=[tm], inc=sC)
        tvv = p.op("vector", lambda e, o=row(1), a=row(1), b=row(2):
                   e.tensor_sub(o, a, b), waits=[tm2], inc=sV)
        tl = p.op("scalar", lambda e, o=row(2), a=row(1), b=c_eps_ln.ap():
                  e.activation(o, a, AF.Ln, bias=b), waits=[tvv], inc=sC)
        tA = p.op("scalar", lambda e, o=row(3), a=row(2), s2=c_nhalf.ap():
                  e.activation(o, a, AF.Exp, scale=s2), inc=sC)
        tB1 = p.op("vector", lambda e, o=row(2), a=row(0), b=row(3):
                   e.tensor_mul(o, a, b), waits=[tA, tl], inc=sV)
        tB2 = p.op("vector", lambda e, o=row(2), a=row(2):
                   e.tensor_scalar_mul(o, a, -1.0), inc=sV)
        tslk = p.op("vector", lambda e, o=SLN: e.tensor_copy(o, o), inc=sV)
        tba = p.op("tensor", lambda e, o=PB[4].ap(), l=ones_k1.ap(), r=row(3):
                   e.matmul(o, l, r, start=True, stop=True),
                   waits=[tA, bc_free], inc=sT)
        tca = p.op("vector", lambda e, o=Abc, a=PB[4].ap(): e.tensor_copy(o, a),
                   waits=[tba], inc=sV)
        tbb = p.op("tensor", lambda e, o=PB[4].ap(), l=ones_k1.ap(), r=row(2):
                   e.matmul(o, l, r, start=True, stop=True),
                   waits=[tslk, tca], inc=sT)
        tcb = p.op("vector", lambda e, o=Bbc, a=PB[4].ap(): e.tensor_copy(o, a),
                   waits=[tbb], inc=sV)
        bc_free = tcb
        for dc in range(DC):
            x1d = Bfap[:, dc * 512:(dc + 1) * 512]
            tv2, tstat = x1_ready[dc]
            to1 = p.op("vector", lambda e, o=Esl(0), a=x1d, b=Abc:
                       e.tensor_mul(o, a, b), waits=[tca, tstat], inc=sV)
            to2 = p.op("vector", lambda e, o=x1d, a=Esl(0), b=Bbc:
                       e.tensor_add(o, a, b), waits=[tcb], inc=sV)
            wr = dma("gpsimd", sG,
                     out[dc * 128:(dc + 1) * 128, t * 512:(t + 1) * 512],
                     x1d, waits=[to2])
            x1_free[dc] = wr
            out_last = wr
            x1_ready[dc] = None

    p.op("sync", None, waits=[t for t in _qstate["gpsimd"]["last"] if t is not None])

    p.replay()
    es.close()
    return nc


_NC_CACHE = {}


def kernel(L0_attn_hidden_states, L0_attn_Wq, L0_attn_bq, L0_attn_Wk,
           L0_attn_bk, L0_attn_Wv, L0_attn_bv, L0_attn_Wo,
           L0_ffn_w1, L0_ffn_b1, L0_ffn_w2, L0_ffn_b2):
    x = np.ascontiguousarray(np.asarray(L0_attn_hidden_states, dtype=np.float32))
    Wq = np.asarray(L0_attn_Wq, dtype=np.float32)
    Wk = np.asarray(L0_attn_Wk, dtype=np.float32)
    Wv = np.asarray(L0_attn_Wv, dtype=np.float32)
    Wo = np.asarray(L0_attn_Wo, dtype=np.float32)
    w1 = np.asarray(L0_ffn_w1, dtype=np.float32)
    w2 = np.asarray(L0_ffn_w2, dtype=np.float32)

    wq_t = np.ascontiguousarray(
        Wq.reshape(16, 128, 16, 128).transpose(0, 3, 2, 1)).reshape(16, 128, 2048)
    wo_t = np.ascontiguousarray(
        Wo.reshape(16, 128, 16, 128).transpose(0, 3, 2, 1)).reshape(16, 128, 2048)
    wk_t = np.ascontiguousarray(Wk.T)
    wv_t = np.ascontiguousarray(Wv.T)
    w1_t = np.ascontiguousarray(
        w1.reshape(64, 128, 16, 128).transpose(0, 3, 2, 1)).reshape(64, 128, 2048)
    w2_t = np.ascontiguousarray(
        w2.reshape(16, 128, 64, 128).transpose(0, 3, 2, 1)).reshape(16, 128, 8192)
    bq_t = np.ascontiguousarray(
        np.asarray(L0_attn_bq, dtype=np.float32).reshape(16, 128).T)
    b1_t = np.ascontiguousarray(
        np.asarray(L0_ffn_b1, dtype=np.float32).reshape(64, 128).T)
    b2_t = np.ascontiguousarray(
        np.asarray(L0_ffn_b2, dtype=np.float32).reshape(16, 128).T)
    bk_t = np.ascontiguousarray(np.asarray(L0_attn_bk, dtype=np.float32)[None, :])
    bv_t = np.ascontiguousarray(np.asarray(L0_attn_bv, dtype=np.float32)[None, :])

    in_maps = []
    for c in range(8):
        b, half = c // 2, c % 2
        xtb = x[b].T
        own = xtb[:, half * 2048:(half + 1) * 2048]
        other = xtb[:, (1 - half) * 2048:(2 - half) * 2048]
        xt_c = np.ascontiguousarray(np.concatenate([own, other], axis=1))
        in_maps.append({
            "xt": xt_c, "wq": wq_t, "wk": wk_t, "wv": wv_t, "wo": wo_t,
            "w1": w1_t, "w2": w2_t, "bq": bq_t, "bk": bk_t, "bv": bv_t,
            "b1": b1_t, "b2": b2_t,
        })

    key = "dbg" if _DEBUG else "nc"
    if key not in _NC_CACHE:
        _NC_CACHE[key] = build_nc()
    res = run_bass_kernel_spmd(_NC_CACHE[key], in_maps, core_ids=list(range(8)))

    outp = np.empty((4, 4096, 2048), dtype=np.float32)
    for c in range(8):
        b, half = c // 2, c % 2
        outp[b, half * 2048:(half + 1) * 2048, :] = res.results[c]["out"].T
    if _DEBUG:
        return outp, res.results
    return outp
